# version 28
# baseline (speedup 1.0000x reference)
import numpy as np
import ml_dtypes

import concourse.bass as bass
import concourse.mybir as mybir
from concourse.bass_utils import run_bass_kernel_spmd

F32 = mybir.dt.float32
BF16 = mybir.dt.bfloat16
AF = mybir.ActivationFunctionType
ALU = mybir.AluOpType
AX = mybir.AxisListType

NCORES = 8
D = 1024
SEQ = 2048
TPB = 16
NB = 2
NT = TPB * NB
NH = 16
HD = 64
WIDTH = 2040
PAR = 8
NNODES = 255
DEPTH = 7
DECW = PAR * (2**DEPTH - 1)
EPS_LN = 1e-5
EPS_AT = 1e-6


def _tree_perm():
    perm = []
    ordd = [(t, 0) for t in range(PAR)]
    for d in range(DEPTH + 1):
        perm.extend(t * NNODES + (2**d - 1 + i) for (t, i) in ordd)
        ordd = [(t, 2 * i) for (t, i) in ordd] + [(t, 2 * i + 1) for (t, i) in ordd]
    return np.array(perm, dtype=np.int64)


class Em:
    def __init__(self, nc):
        self.nc = nc
        self.eng = {"T": nc.tensor, "V": nc.vector, "S": nc.scalar,
                    "Y": nc.sync, "G": nc.gpsimd}
        self.sems = {}
        self.cnt = {}
        self.seen = {e: {} for e in self.eng}
        self.war = {}

    def sem(self, name):
        h = self.nc.semaphore(name).__enter__()
        self.sems[name] = h
        self.cnt[name] = 0
        for e in self.seen:
            self.seen[e][name] = 0

    def wait(self, e, *evs):
        for ev in evs:
            if ev is None:
                continue
            s, v = ev
            if v <= self.seen[e][s]:
                continue
            self.eng[e].wait_ge(self.sems[s], v)
            self.seen[e][s] = v

    def sig(self, inst, s, k=1):
        self.cnt[s] += k
        inst.then_inc(self.sems[s], k)
        return (s, self.cnt[s])


def build_nc():
    nc = bass.Bass("TRN2", target_bir_lowering=False, debug=False)

    x_d = nc.dram_tensor("x", [NT * 128, D], F32, kind="ExternalInput").ap()
    y_d = nc.dram_tensor("y", [NT * 128, D], F32, kind="ExternalOutput").ap()
    wq_d = nc.dram_tensor("wqt", [D, D], F32, kind="ExternalInput").ap()
    wk_d = nc.dram_tensor("wkt", [D, D], F32, kind="ExternalInput").ap()
    wv_d = nc.dram_tensor("wvt", [D, D], F32, kind="ExternalInput").ap()
    wo_d = nc.dram_tensor("wot", [D, D], F32, kind="ExternalInput").ap()
    win_d = nc.dram_tensor("wint", [D, WIDTH], F32, kind="ExternalInput").ap()
    wout_d = nc.dram_tensor("woutt", [WIDTH + 1, D], F32, kind="ExternalInput").ap()
    bqkv_d = nc.dram_tensor("bqkv", [1, 3 * D], F32, kind="ExternalInput").ap()
    bin_d = nc.dram_tensor("binp", [1, WIDTH], F32, kind="ExternalInput").ap()

    ident_d = nc.inline_tensor(np.eye(128, dtype=ml_dtypes.bfloat16), name="identc").ap()
    allone_d = nc.inline_tensor(np.ones((128, 128), dtype=ml_dtypes.bfloat16), name="allonec").ap()

    al = nc.alloc_sbuf_tensor
    s_wkv = al("s_wkv", [128, 16384], BF16).ap()
    s_wq = al("s_wq", [128, 8192], BF16).ap()
    s_wo = al("s_wo", [128, 8192], BF16).ap()
    s_win = al("s_win", [128, 8 * WIDTH], BF16).ap()
    s_ident = al("s_ident", [128, 128], BF16).ap()
    s_allone = al("s_allone", [128, 128], BF16).ap()
    s_bq_b = al("s_bq_b", [128, D], BF16).ap()
    s_bk_b = al("s_bk_b", [128, D], BF16).ap()
    s_bv_b = al("s_bv_b", [128, D], BF16).ap()
    s_bin_b = al("s_bin_b", [128, WIDTH], BF16).ap()
    s_cns = al("s_cns", [128, 2], F32).ap()
    s_kvb = [al(f"s_kvb{b}", [128, 512], BF16).ap() for b in range(NB)]
    s_ksb = [al(f"s_ksb{b}", [128, D], F32).ap() for b in range(NB)]
    s_x = [al(f"s_x{i}", [128, D], F32).ap() for i in range(4)]
    s_xn = [al(f"s_xn{i}", [128, D], BF16).ap() for i in range(2)]
    s_xnt = [al(f"s_xnt{i}", [128, D], BF16).ap() for i in range(2)]
    s_lnp = al("s_lnp", [128, 16], F32).ap()
    s_stats = al("s_stats", [128, 2 * NT], F32).ap()
    s_scr1 = al("s_scr1", [128, D], F32).ap()
    s_scr2 = al("s_scr2", [128, D], F32).ap()
    s_phik = [al(f"s_phik{i}", [128, D], BF16).ap() for i in range(2)]
    s_v = [al(f"s_v{i}", [128, D], BF16).ap() for i in range(2)]
    s_phiq = al("s_phiq", [128, D], F32).ap()
    s_qp = al("s_qp", [128, D], BF16).ap()
    s_qpt = al("s_qpt", [128, D], BF16).ap()
    s_ap = al("s_ap", [128, D], BF16).ap()
    s_acts = [al(f"s_acts{i}", [128, WIDTH], BF16).ap() for i in range(2)]
    s_dn = [al(f"s_dn{i}", [128, 2040], BF16).ap() for i in range(2)]
    s_masked = al("s_masked", [128, WIDTH], BF16).ap()
    s_mT = al("s_mT", [128, 2048], BF16).ap()
    s_norm = al("s_norm", [128, 32], F32).ap()
    s_wtmp = al("s_wtmp", [128, D], F32).ap()
    s_out = [al(f"s_out{i}", [128, D], F32).ap() for i in range(2)]

    em = Em(nc)
    for s in ("sc", "sx0", "sx1", "sx2", "sx3", "so0", "so1", "swbr",
              "swk", "swv", "swq", "swo", "swin", "swout",
              "st", "sv", "ss"):
        em.sem(s)
    T, V, S, Y, G = (em.eng[e] for e in "TVSYG")

    for dst, srcd in ((s_ident, ident_d), (s_allone, allone_d)):
        i = Y.dma_start(out=dst, in_=srcd)
        ev_consts = em.sig(i, "sc", 16)
    ev_ident = ev_allone = ev_consts

    def wdma(dst, src, sem):
        i = G.dma_start(out=dst, in_=src)
        return em.sig(i, sem, 16)

    wdma(s_bq_b[0:1, :], bqkv_d[0:1, 0:1024], "swbr")
    wdma(s_bk_b[0:1, :], bqkv_d[0:1, 1024:2048], "swbr")
    wdma(s_bv_b[0:1, :], bqkv_d[0:1, 2048:3072], "swbr")
    ev_brows = wdma(s_bin_b[0:1, :], bin_d, "swbr")

    def chunked(dst_cols0, src, ncols, nch, s_dst, sem):
        dst = s_dst[:, dst_cols0:dst_cols0 + nch * ncols].rearrange(
            "p (c j) -> p c j", c=nch)
        srcv = src.rearrange("(c p) j -> p c j", p=128)
        return wdma(dst, srcv, sem)

    ev_wk = chunked(0, wk_d, D, 8, s_wkv, "swk")
    ev_wv = chunked(8192, wv_d, D, 8, s_wkv, "swv")
    ev_wq = chunked(0, wq_d, D, 8, s_wq, "swq")
    ev_wo = chunked(0, wo_d, D, 8, s_wo, "swo")
    ev_win = chunked(0, win_d, WIDTH, 8, s_win, "swin")

    i = V.memset(s_mT[96:128, 1920:2048], 1.0)
    ev_mtones = em.sig(i, "sv")
    V.memset(s_cns[:, 0:1], 1.0 / D)
    i = V.memset(s_cns[:, 1:2], EPS_LN)
    ev_cns = em.sig(i, "sv")

    p_kq_g = nc.psum_tensor("p_kq", [128, D], F32); p_kq = p_kq_g.__enter__().ap()
    p_v0_g = nc.psum_tensor("p_v0", [128, 512], F32); p_v0 = p_v0_g.__enter__().ap()
    p_kv_g = nc.psum_tensor("p_kv", [128, D], F32); p_kv = p_kv_g.__enter__().ap()
    p_ksb_g = nc.psum_tensor("p_ksb", [128, D], F32); p_ksb = p_ksb_g.__enter__().ap()
    p_xnt_g = nc.psum_tensor("p_xnt", [128, D], BF16); p_xnt = p_xnt_g.__enter__().ap()

    ev_bc_v = None
    for dst, c0, c1 in ((s_bq_b, 0, 1024), (s_bk_b, 0, 1024), (s_bv_b, 0, 1024),
                        (s_bin_b, 0, 1024), (s_bin_b, 1024, 2040)):
        em.wait("T", ev_brows, ev_consts, ev_bc_v)
        w = c1 - c0
        T.matmul(p_kq[:, 0:512], s_allone[0:1, :], dst[0:1, c0:c0 + 512],
                 start=True, stop=True)
        i = T.matmul(p_kq[:, 512:512 + (w - 512)], s_allone[0:1, :],
                     dst[0:1, c0 + 512:c1], start=True, stop=True)
        ev_bc_mm = em.sig(i, "st")
        em.wait("V", ev_bc_mm)
        i = V.tensor_copy(dst[:, c0:c1], p_kq[:, 0:w])
        ev_bc_v = em.sig(i, "sv")
    ev_biasb = ev_bc_v

    war_x = [None] * 4
    war_xn = [None, None]
    war_xnt = [None, None]
    war_ln = [None, None]

    def emit_xload(t):
        xb = t % 4
        tok0 = (t % NT) * 128
        em.wait("Y", war_x[xb])
        i = Y.dma_start(out=s_x[xb], in_=x_d[tok0:tok0 + 128, :])
        return em.sig(i, f"sx{xb}", 16)

    def emit_ln(t, ev_x):
        buf = t % 2
        xb = t % 4
        i_t = t % NT
        em.wait("V", ev_x, war_xn[buf])
        i = V.tensor_scalar(s_xn[buf], s_x[xb], s_stats[:, 2 * i_t:2 * i_t + 1],
                            s_stats[:, 2 * i_t + 1:2 * i_t + 2],
                            ALU.subtract, ALU.mult)
        ev = em.sig(i, "sv")
        war_x[xb] = ev
        return ev

    war_lnp = [None]

    def emit_sums(t, ev_x):
        p = t % 2
        xb = t % 4
        em.wait("S", ev_x, em.war.get(("sq",)), war_lnp[0])
        i = S.activation(s_phiq, s_x[xb], AF.Square,
                         accum_out=s_lnp[:, 2 + p:3 + p])
        ev_ss = em.sig(i, "ss")
        em.war[("sq",)] = ev_ss
        em.wait("V", ev_x)
        i = V.reduce_sum(s_lnp[:, p:p + 1], s_x[xb], axis=AX.X)
        ev_sv = em.sig(i, "sv")
        return (ev_sv, ev_ss)

    def emit_pair(u, se0, se1):
        pstep = s_stats.ap[0][0]
        mean_ap = bass.AP(tensor=s_stats.tensor,
                          offset=s_stats.offset + 2 * u,
                          ap=[[pstep, 128], [2, 2]])
        rstd_ap = bass.AP(tensor=s_stats.tensor,
                          offset=s_stats.offset + 2 * u + 1,
                          ap=[[pstep, 128], [2, 2]])
        em.wait("V", se0[0], se0[1], se1[0], se1[1])
        i = V.tensor_scalar(mean_ap, s_lnp[:, 0:2], 1.0 / D, None, ALU.mult)
        e = em.sig(i, "sv")
        em.wait("V", e)
        i = V.tensor_tensor(s_lnp[:, 4:6], s_lnp[:, 0:2], mean_ap, ALU.mult)
        e = em.sig(i, "sv")
        em.wait("V", e, em.war.get(("lnp68",)))
        i = V.tensor_tensor(s_lnp[:, 6:8], s_lnp[:, 2:4], s_lnp[:, 4:6],
                            ALU.subtract)
        e3 = em.sig(i, "sv")
        war_lnp[0] = e3
        em.wait("S", e3, ev_cns)
        i = S.activation(s_lnp[:, 8:10], s_lnp[:, 6:8], AF.Sqrt,
                         scale=s_cns[:, 0:1], bias=s_cns[:, 1:2])
        e4 = em.sig(i, "ss")
        em.war[("lnp68",)] = e4
        em.wait("V", e4)
        i = V.reciprocal(rstd_ap, s_lnp[:, 8:10])
        return em.sig(i, "sv")

    def emit_xnstt(u, ev_y):
        xb = u % 4
        buf = u % 2
        em.wait("V", ev_y, war_xn[buf])
        i = V.tensor_scalar(s_xn[buf], s_x[xb],
                            s_stats[:, 2 * u:2 * u + 1],
                            s_stats[:, 2 * u + 1:2 * u + 2],
                            ALU.subtract, ALU.mult)
        ev = em.sig(i, "sv")
        war_x[xb] = ev
        return ev

    def emit_xnt(t, ev_xn, war_bank, p_dst):
        buf = t % 2
        em.wait("T", ev_xn, ev_ident, war_bank)
        for c in range(8):
            i = T.transpose(p_dst[:, 128 * c:128 * (c + 1)],
                            s_xn[buf][:, 128 * c:128 * (c + 1)], s_ident)
        ev_tr = em.sig(i, "st")
        war_xn[buf] = ev_tr
        war_ln[buf] = ev_tr
        em.wait("S", ev_tr, war_xnt[buf])
        i = S.copy(s_xnt[buf], p_dst[:, 0:1024])
        ev = em.sig(i, "ss")
        return ev, ev_tr

    kv_evac = [None, None]
    ksb_evac = [None, None]
    war_phik = [None, None]
    war_v = [None, None]
    war_pxnt = None
    pending_kv = None

    ev_xl = [None] * NT
    sums_ev = [None] * NT
    ev_stt = [None] * NT
    ev_xl[0] = emit_xload(0)
    sums_ev[0] = emit_sums(0, ev_xl[0])
    ev_xl[1] = emit_xload(1)
    sums_ev[1] = emit_sums(1, ev_xl[1])
    ev_y = emit_pair(0, sums_ev[0], sums_ev[1])
    ev_stt[0] = emit_xnstt(0, ev_y)
    ev_stt[1] = emit_xnstt(1, ev_y)
    ev_xnTe, _ = emit_xnt(0, ev_stt[0], None, p_xnt)
    war_pxnt = ev_xnTe
    ev_xl[2] = emit_xload(2)
    sums_ev[2] = emit_sums(2, ev_xl[2])
    ev_xl[3] = emit_xload(3)

    def emit_kv(pend):
        tt, pbuf, ev_pk, ev_pv = pend
        bb = tt // TPB
        firstt = (tt % TPB) == 0
        lastt = (tt % TPB) == TPB - 1
        em.wait("T", ev_pk, ev_pv)
        if firstt and bb == 1:
            em.wait("T", kv_evac[0], ksb_evac[0])
        for h in range(NH):
            a = h % 2
            j = h // 2
            i = T.matmul(p_kv[64 * a:64 * a + 64,
                              512 * a + 64 * j:512 * a + 64 * (j + 1)],
                         s_phik[pbuf][:, 64 * h:64 * h + 64],
                         s_v[pbuf][:, 64 * h:64 * h + 64],
                         start=(firstt and h < 2), stop=(lastt and h >= NH - 2),
                         tile_position=(0, 64 * a))
        ev_kvv = em.sig(i, "st")
        em.wait("T", ev_allone)
        T.matmul(p_ksb[:, 0:512], s_allone, s_phik[pbuf][:, 0:512],
                 start=firstt, stop=lastt)
        i = T.matmul(p_ksb[:, 512:1024], s_allone, s_phik[pbuf][:, 512:1024],
                     start=firstt, stop=lastt)
        ev_ksbb = em.sig(i, "st")
        war_phik[pbuf] = ev_ksbb
        war_v[pbuf] = ev_kvv
        if lastt:
            em.wait("V", ev_kvv)
            V.tensor_copy(s_kvb[bb][0:64, 0:512], p_kv[0:64, 0:512])
            i = V.tensor_copy(s_kvb[bb][64:128, 0:512], p_kv[64:128, 512:1024])
            kv_evac[bb] = em.sig(i, "sv")
            em.wait("V", ev_ksbb)
            i = V.tensor_copy(s_ksb[bb], p_ksb)
            ksb_evac[bb] = em.sig(i, "sv")

    BNK = [p_kq[:, 0:512], p_kq[:, 512:1024], p_v0]
    war_bank = [ev_biasb, ev_biasb, None]
    ev_last_kvmm = None
    for t in range(NT):
        buf = t % 2
        r0, r1, r2 = t % 3, (t + 1) % 3, (t + 2) % 3

        em.wait("T", ev_xnTe, ev_wk, war_bank[r0], war_bank[r1])
        ev_kh = []
        for half, bk in ((0, BNK[r0]), (1, BNK[r1])):
            for c in range(8):
                i = T.matmul(bk, s_xnt[buf][:, 128 * c:128 * (c + 1)],
                             s_wkv[:, 1024 * c + 512 * half:1024 * c + 512 * (half + 1)],
                             start=(c == 0), stop=(c == 7))
            ev_kh.append(em.sig(i, "st"))
        em.wait("V", ev_kh[0], ev_biasb, em.war.get(("scr1",)))
        i = V.tensor_tensor(s_scr1[:, 0:512], BNK[r0], s_bk_b[:, 0:512], ALU.add)
        ev_kb0 = em.sig(i, "sv")
        war_bank[r0] = ev_kb0
        em.wait("V", ev_kh[1])
        i = V.tensor_tensor(s_scr1[:, 512:1024], BNK[r1], s_bk_b[:, 512:1024],
                            ALU.add)
        ev_kb1 = em.sig(i, "sv")
        war_bank[r1] = ev_kb1

        if pending_kv is not None:
            emit_kv(pending_kv)
            pending_kv = None

        if t + 1 < NT:
            ev_xnTe_next, _ = emit_xnt(t + 1, ev_stt[t + 1], war_pxnt, p_xnt)
            war_pxnt = ev_xnTe_next

        em.wait("T", ev_wv, war_bank[r2])
        for c in range(8):
            i = T.matmul(BNK[r2], s_xnt[buf][:, 128 * c:128 * (c + 1)],
                         s_wkv[:, 8192 + 1024 * c:8192 + 1024 * c + 512],
                         start=(c == 0), stop=(c == 7))
        ev_vh0 = em.sig(i, "st")
        em.wait("T", war_bank[r0])
        for c in range(8):
            i = T.matmul(BNK[r0], s_xnt[buf][:, 128 * c:128 * (c + 1)],
                         s_wkv[:, 8192 + 1024 * c + 512:8192 + 1024 * (c + 1)],
                         start=(c == 0), stop=(c == 7))
        ev_vh1 = em.sig(i, "st")
        ev_vh = [ev_vh0, ev_vh1]
        war_xnt[buf] = ev_vh1
        ev_last_kvmm = ev_vh1

        em.wait("S", ev_kb1, em.war.get(("scr2",)))
        i = S.activation(s_scr2, s_scr1, AF.Relu, scale=-1.0)
        ev_e1 = em.sig(i, "ss")
        em.wait("S", ev_e1)
        i = S.activation(s_scr2, s_scr2, AF.Exp, scale=-1.0)
        ev_e2 = em.sig(i, "ss")

        em.wait("V", ev_vh[0], war_v[buf])
        i = V.tensor_tensor(s_v[buf][:, 0:512], BNK[r2], s_bv_b[:, 0:512],
                            ALU.add)
        war_bank[r2] = em.sig(i, "sv")
        em.wait("V", ev_vh[1])
        i = V.tensor_tensor(s_v[buf][:, 512:1024], BNK[r0], s_bv_b[:, 512:1024],
                            ALU.add)
        ev_vb = em.sig(i, "sv")
        war_bank[r0] = ev_vb

        em.wait("V", ev_e2, war_phik[buf])
        i = V.scalar_tensor_tensor(s_phik[buf], s_scr1, 0.0, s_scr2,
                                   ALU.max, ALU.add)
        ev_phik = em.sig(i, "sv")
        em.war[("scr1",)] = ev_phik
        em.war[("scr2",)] = ev_phik
        pending_kv = (t, buf, ev_phik, ev_vb)

        if t + 3 < NT:
            sums_ev[t + 3] = emit_sums(t + 3, ev_xl[t + 3])
            if t % 2 == 0:
                ev_y = emit_pair(t + 2, sums_ev[t + 2], sums_ev[t + 3])
                ev_stt[t + 2] = emit_xnstt(t + 2, ev_y)
                ev_stt[t + 3] = emit_xnstt(t + 3, ev_y)

        if t + 1 < NT:
            ev_xnTe = ev_xnTe_next
        if t + 4 < NT:
            ev_xl[t + 4] = emit_xload(t + 4)

    emit_kv(pending_kv)
    pending_kv = None

    em.wait("G", ev_last_kvmm)
    chunked(0, wout_d[0:1920, :], D, 15, s_wkv, "swout")
    i = G.dma_start(out=s_wkv[0:121, 15360:16384], in_=wout_d[1920:2041, :])
    ev_wout = em.sig(i, "swout", 16)

    p_xnt_g.__exit__(None, None, None)
    p_ksb_g.__exit__(None, None, None)
    p_kv_g.__exit__(None, None, None)
    p_v0_g.__exit__(None, None, None)
    p_kq_g.__exit__(None, None, None)

    p_acc = [nc.psum_tensor(f"p_acc{i}", [128, D], F32).__enter__().ap()
             for i in range(2)]
    p_L = nc.psum_tensor("p_L", [128, D], F32).__enter__().ap()
    p_tr = nc.psum_tensor("p_tr", [128, 2048], BF16).__enter__().ap()

    em.wait("T", war_bank[0], war_bank[1], war_bank[2], kv_evac[1],
            ksb_evac[1], war_pxnt,
            war_phik[0], war_phik[1], war_v[0], war_v[1])
    war_acc = [[], []]
    war_Lp = []
    ev_prev_out = [None, None]
    war_scr = {"scr2": em.war.get(("scr2",)),
               "phiq": em.war.get(("sq",))}
    _rn = s_norm[:, 16:32]
    rn_ap = bass.AP(tensor=_rn.tensor, offset=_rn.offset,
                    ap=[[_rn.ap[0][0], 128], [_rn.ap[1][0], NH], [0, HD]])
    srcL = bass.AP(tensor=p_L.tensor, offset=p_L.offset,
                   ap=[[p_L.ap[0][0], 128], [512, 2], [1, 510]])

    def emit_qmm(t, ev_xnTe_t):
        acc = p_acc[t % 2]
        buf = t % 2
        em.wait("T", ev_xnTe_t, ev_wq, *war_acc[t % 2])
        for c in range(8):
            lhs = s_xnt[buf][:, 128 * c:128 * (c + 1)]
            T.matmul(acc[:, 0:512], lhs, s_wq[:, 1024 * c:1024 * c + 512],
                     start=(c == 0), stop=(c == 7))
            i = T.matmul(acc[:, 512:1024], lhs, s_wq[:, 1024 * c + 512:1024 * (c + 1)],
                         start=(c == 0), stop=(c == 7))
        return em.sig(i, "st")

    def emit_qbias(t, ev_mm):
        acc = p_acc[t % 2]
        em.wait("V", ev_mm, ev_biasb)
        i = V.tensor_tensor(acc, acc, s_bq_b, ALU.add)
        return em.sig(i, "sv")

    def emit_logits(t, phase):
        buf = t % 2
        em.wait("T", ev_win, *war_Lp)
        for c in range(8):
            lhs = s_xnt[buf][:, 128 * c:128 * (c + 1)]
            for k4 in (0, 1):
                kk = 2 * phase + k4
                T.matmul(p_L[:, 512 * k4:512 * k4 + 510], lhs,
                         s_win[:, WIDTH * c + 510 * kk:WIDTH * c + 510 * (kk + 1)],
                         start=(c == 0), stop=False)
        for k4 in (0, 1):
            kk = 2 * phase + k4
            i = T.matmul(p_L[:, 512 * k4:512 * k4 + 510], s_allone[0:1, :],
                         s_bin_b[0:1, 510 * kk:510 * (kk + 1)],
                         start=False, stop=True)
        return em.sig(i, "st")

    def emit_ffn_tanh(t, ev_Lmm, phase):
        pb = t % 2
        em.wait("S", ev_Lmm, war_scr.get("acts"))
        i = S.activation(
            s_acts[pb][:, 1020 * phase:1020 * (phase + 1)].rearrange(
                "p (b c) -> p b c", b=2), srcL, AF.Tanh, scale=0.5)
        return em.sig(i, "ss")

    def emit_ffn_acts(t, ev_th, phase):
        pb = t % 2
        evs = []
        if phase == 0:
            em.wait("V", ev_th, war_scr.get("dec"))
            i = V.tensor_scalar(s_dn[pb][:, 1020:2040], s_acts[pb][:, 0:1020],
                                0.0, None, ALU.is_gt)
            ev_dec = em.sig(i, "sv")
            em.wait("V", war_scr.get("nd"))
            i = V.tensor_scalar(s_dn[pb][:, 0:1020], s_acts[pb][:, 0:1020],
                                0.0, None, ALU.is_le)
            ev_nd = em.sig(i, "sv")
            evs.extend([ev_dec, ev_nd])
        em.wait("V", ev_th)
        sl = s_acts[pb][:, 1020 * phase:1020 * (phase + 1)].rearrange(
            "p (b c) -> p b c", b=2)
        i = V.scalar_tensor_tensor(sl, sl, 1.0, srcL, ALU.add, ALU.mult)
        ev_acts = em.sig(i, "sv")
        return [ev_acts] + evs

    def emit_echain(t, ev_qmm_t):
        em.wait("S", ev_qmm_t, war_scr.get("scr2"))
        i = S.activation(s_scr2, p_acc[t % 2], AF.Relu, scale=-1.0)
        ev_e1 = em.sig(i, "ss")
        em.wait("S", ev_e1)
        i = S.activation(s_scr2, s_scr2, AF.Exp, scale=-1.0)
        return em.sig(i, "ss")

    def rn_ap_h(h5):
        return bass.AP(tensor=s_norm.tensor, offset=s_norm.offset + 16 + 8 * h5,
                       ap=[[s_norm.ap[0][0], 128], [1, 8], [0, HD]])

    def emit_phiq_half(t, h5, ev_e2_t, ev_qmm_t):
        acc = p_acc[t % 2]
        bb = t // TPB
        hc = 512 * h5
        if h5 == 0:
            em.wait("V", ev_e2_t, ev_qmm_t, war_scr.get("phiq"))
        i = V.scalar_tensor_tensor(s_phiq[:, hc:hc + 512], acc[:, hc:hc + 512],
                                   0.0, s_scr2[:, hc:hc + 512],
                                   ALU.max, ALU.add)
        ev_phiq = em.sig(i, "sv")
        em.wait("V", ev_phiq, ksb_evac[bb], em.war.get(("scr1",)))
        i = V.tensor_tensor(s_scr1[:, hc:hc + 512], s_phiq[:, hc:hc + 512],
                            s_ksb[bb][:, hc:hc + 512], ALU.mult)
        ev_np = em.sig(i, "sv")
        em.wait("V", ev_np)
        i = V.reduce_sum(
            s_norm[:, 8 * h5:8 * h5 + 8].rearrange("p (h o) -> p h o", o=1),
            s_scr1[:, hc:hc + 512].rearrange("p (h d) -> p h d", h=8),
            axis=AX.X)
        ev_seg = em.sig(i, "sv")
        if h5 == 1:
            em.war[("scr1",)] = ev_seg
        em.wait("V", ev_seg)
        i = V.tensor_scalar(s_norm[:, 16 + 8 * h5:24 + 8 * h5],
                            s_norm[:, 8 * h5:8 * h5 + 8], EPS_AT, None, ALU.max)
        e = em.sig(i, "sv")
        em.wait("V", e)
        i = V.reciprocal(s_norm[:, 16 + 8 * h5:24 + 8 * h5],
                         s_norm[:, 16 + 8 * h5:24 + 8 * h5])
        ev_rn = em.sig(i, "sv")
        em.wait("V", ev_rn, war_scr.get("qp"))
        i = V.tensor_tensor(
            s_qp[:, hc:hc + 512].rearrange("p (h d) -> p h d", h=8),
            s_phiq[:, hc:hc + 512].rearrange("p (h d) -> p h d", h=8),
            rn_ap_h(h5), ALU.mult)
        ev_qp = em.sig(i, "sv")
        return ev_phiq, ev_qp

    def emit_phiq_pair(t, ev_e2, ev_qb):
        ev_phiq0, ev_qp0 = emit_phiq_half(t, 0, ev_e2, ev_qb)
        ev_phiq1, ev_qp1 = emit_phiq_half(t, 1, None, None)
        war_scr["scr2"] = ev_phiq1
        war_scr["phiq"] = ev_qp1
        return {"phiq0": ev_phiq0, "qp0": ev_qp0,
                "phiq1": ev_phiq1, "qp1": ev_qp1}

    def emit_mask_levels(t, dec_nd):
        pb = t % 2
        dnb = s_dn[pb]
        pstep = s_masked.ap[0][0]
        dnstep = dnb.ap[0][0]
        em.wait("V", *dec_nd)
        em.wait("V", war_scr.get("masked"))
        i = V.tensor_scalar(
            bass.AP(tensor=s_masked.tensor, offset=s_masked.offset + 8,
                    ap=[[pstep, 128], [8, 2], [1, 8]]),
            bass.AP(tensor=dnb.tensor, offset=dnb.offset,
                    ap=[[dnstep, 128], [1020, 2], [1, 8]]),
            0.5, None, ALU.mult)
        ev_m = em.sig(i, "sv")
        for d in range(1, DEPTH):
            sdd = PAR * 2**d
            o = PAR * (2**d - 1)
            o2 = PAR * (2**(d + 1) - 1)
            em.wait("V", ev_m)
            i = V.tensor_tensor(
                bass.AP(tensor=s_masked.tensor, offset=s_masked.offset + o2,
                        ap=[[pstep, 128], [sdd, 2], [1, sdd]]),
                bass.AP(tensor=s_masked.tensor, offset=s_masked.offset + o,
                        ap=[[pstep, 128], [0, 2], [1, sdd]]),
                bass.AP(tensor=dnb.tensor, offset=dnb.offset + o,
                        ap=[[dnstep, 128], [1020, 2], [1, sdd]]),
                ALU.mult)
            ev_m = em.sig(i, "sv")
        return ev_m

    def emit_mask_final(t, ev_lev, silu_evs):
        pb = t % 2
        actsb = s_acts[pb]
        em.wait("V", ev_lev, *silu_evs)
        i = V.tensor_tensor(s_masked[:, 8:WIDTH], actsb[:, 8:WIDTH],
                            s_masked[:, 8:WIDTH], ALU.mult)
        e = em.sig(i, "sv")
        em.wait("V", e)
        i = V.tensor_scalar(s_masked[:, 0:8], actsb[:, 0:8], 0.5, None, ALU.mult)
        ev_mk = em.sig(i, "sv")
        war_scr["acts"] = ev_mk
        war_scr["dec"] = ev_mk
        war_scr["nd"] = ev_mk
        return ev_mk

    ev_x = emit_xload(NT + 0)
    ln_cur = emit_ln(NT + 0, ev_x)
    ev_xnTe0, _ = emit_xnt(NT + 0, ln_cur, None, p_tr)
    ev_x2 = emit_xload(NT + 1)
    ln1 = emit_ln(NT + 1, ev_x2)
    ev_qmm0 = emit_qmm(0, ev_xnTe0)
    ev_L1 = emit_logits(0, 0)
    ev_qb0 = emit_qbias(0, ev_qmm0)
    ev_e2_0 = emit_echain(0, ev_qb0)
    th1 = emit_ffn_tanh(0, ev_L1, 0)
    av1 = emit_ffn_acts(0, th1, 0)
    war_Lp = [th1, av1[0]]
    ev_L2 = emit_logits(0, 1)
    th2 = emit_ffn_tanh(0, ev_L2, 1)
    av2 = emit_ffn_acts(0, th2, 1)
    war_Lp = [th2, av2[0]]
    mlev0 = emit_mask_levels(0, av1[1:])
    phi0 = emit_phiq_pair(0, ev_e2_0, ev_qb0)
    xnTe_next, _ = emit_xnt(NT + 1, ln1, ev_xnTe0, p_tr)
    ev_x3 = emit_xload(NT + 2)
    ln_for_xnt = emit_ln(NT + 2, ev_x3)
    cur = dict(mlev=mlev0, silu=[av1[0], av2[0]], **phi0)
    war_trhi = None

    for t in range(NT):
        b = t // TPB
        buf = t % 2
        tok0 = t * 128
        acc = p_acc[t % 2]

        nxt = {}
        if t + 1 < NT:
            nxt["ev_qmm"] = emit_qmm(t + 1, xnTe_next)
            nxt["_L1"] = emit_logits(t + 1, 0)

        ev_phiq0 = cur["phiq0"]; ev_qp0 = cur["qp0"]
        ev_phiq1 = cur["phiq1"]; ev_qp1 = cur["qp1"]
        ev_mk = emit_mask_final(t, cur["mlev"], cur["silu"])

        em.wait("T", ev_qp0, xnTe_next)
        for c in range(4):
            i = T.transpose(p_tr[:, 128 * c:128 * (c + 1)],
                            s_qp[:, 128 * c:128 * (c + 1)], s_ident)
        ev_qpT0 = em.sig(i, "st")
        em.wait("T", ev_qp1, war_trhi)
        for c in range(4, 8):
            i = T.transpose(p_tr[:, 1024 + 128 * (c - 4):1024 + 128 * (c - 3)],
                            s_qp[:, 128 * c:128 * (c + 1)], s_ident)
        ev_qpT1 = em.sig(i, "st")
        war_scr["qp"] = ev_qpT1
        em.wait("S", ev_qpT0, war_scr.get("qpt"))
        i = S.copy(s_qpt[:, 0:512], p_tr[:, 0:512])
        ev_qpTe0 = em.sig(i, "ss")
        em.wait("S", ev_qpT1)
        i = S.copy(s_qpt[:, 512:1024], p_tr[:, 1024:1536])
        ev_qpTe1 = em.sig(i, "ss")

        em.wait("T", ev_qpTe0, kv_evac[b], ev_phiq0)
        for h in range(0, 8):
            a = h % 2
            j = h // 2
            i = T.matmul(acc[64 * a:64 * a + 64, 128 * j:128 * (j + 1)],
                         s_kvb[b][64 * a:64 * a + 64, 64 * j:64 * j + 64],
                         s_qpt[64 * a:64 * a + 64, 128 * j:128 * (j + 1)],
                         start=True, stop=True, tile_position=(64 * a, 64 * a))
        ev_ap0 = em.sig(i, "st")
        em.wait("T", ev_qpTe1, ev_phiq1)
        for h in range(8, NH):
            a = h % 2
            j = h // 2
            i = T.matmul(acc[64 * a:64 * a + 64, 128 * j:128 * (j + 1)],
                         s_kvb[b][64 * a:64 * a + 64, 64 * j:64 * j + 64],
                         s_qpt[64 * a:64 * a + 64, 128 * j:128 * (j + 1)],
                         start=True, stop=True, tile_position=(64 * a, 64 * a))
        ev_ap1 = em.sig(i, "st")
        war_scr["qpt"] = ev_ap1
        em.wait("S", ev_ap0, war_scr.get("ap"))
        i = S.copy(s_ap[:, 0:512], acc[:, 0:512])
        em.sig(i, "ss")
        em.wait("S", ev_ap1)
        i = S.copy(s_ap[:, 512:1024], acc[:, 512:1024])
        ev_ape = em.sig(i, "ss")

        em.wait("T", ev_mk, ev_qpTe1)
        for jc in range(15):
            i = T.transpose(p_tr[:, 128 * jc:128 * (jc + 1)],
                            s_masked[:, 128 * jc:128 * (jc + 1)], s_ident)
        i = T.transpose(p_tr[0:120, 1920:2048], s_masked[:, 1920:WIDTH], s_ident)
        ev_mTmm = em.sig(i, "st")
        war_scr["masked"] = ev_mTmm
        em.wait("S", ev_mTmm, ev_mtones, war_scr.get("mT"))
        i = S.copy(s_mT[:, 0:1024], p_tr[:, 0:1024])
        ev_mTe0 = em.sig(i, "ss")
        S.copy(s_mT[:, 1024:1920], p_tr[:, 1024:1920])
        i = S.copy(s_mT[0:120, 1920:2048], p_tr[0:120, 1920:2048])
        ev_mTe1 = em.sig(i, "ss")
        war_trhi = ev_mTe1

        em.wait("T", ev_ape, ev_wo)
        for c in range(8):
            lhs = s_ap[:, 128 * c:128 * (c + 1)]
            T.matmul(acc[:, 0:512], lhs, s_wo[:, 1024 * c:1024 * c + 512],
                     start=(c == 0), stop=False)
            i = T.matmul(acc[:, 512:1024], lhs, s_wo[:, 1024 * c + 512:1024 * (c + 1)],
                         start=(c == 0), stop=False)
        ev_womm = em.sig(i, "st")
        war_scr["ap"] = ev_womm

        if t + 1 < NT:
            nxt["ev_qb"] = emit_qbias(t + 1, nxt["ev_qmm"])
            nxt["ev_e2"] = emit_echain(t + 1, nxt["ev_qb"])
            nxt["_th1"] = emit_ffn_tanh(t + 1, nxt["_L1"], 0)

        if t + 1 < NT:
            av = emit_ffn_acts(t + 1, nxt["_th1"], 0)
            nxt["silu"] = [av[0]]
            war_Lp = [nxt["_th1"], av[0]]
            nxt["mlev"] = emit_mask_levels(t + 1, av[1:])
            nxt.update(emit_phiq_pair(t + 1, nxt["ev_e2"], nxt["ev_qb"]))

        if t + 2 < NT:
            nxt["xnTe_next"], _ = emit_xnt(NT + t + 2, ln_for_xnt, ev_mTe0, p_tr)

        em.wait("T", ev_mTe0, ev_wout)
        ev_oh = []
        for half in range(2):
            for j in range(16):
                if j == 8:
                    em.wait("T", ev_mTe1)
                kj = 128 if j < 15 else 121
                lhs = s_mT[0:kj, 128 * j:128 * (j + 1)]
                i = T.matmul(acc[:, 512 * half:512 * (half + 1)], lhs,
                             s_wkv[0:kj, 1024 * j + 512 * half:
                                   1024 * j + 512 * (half + 1)],
                             start=False, stop=(j == 15))
            ev_oh.append(em.sig(i, "st"))
        war_scr["mT"] = ev_oh[1]

        em.wait("S", ev_oh[0], war_scr.get("wtmp"))
        i = S.copy(s_wtmp[:, 0:512], acc[:, 0:512])
        ev_c0 = em.sig(i, "ss")
        em.wait("S", ev_oh[1])
        i = S.copy(s_wtmp[:, 512:1024], acc[:, 512:1024])
        ev_c1 = em.sig(i, "ss")
        war_acc[t % 2] = [ev_c1]
        em.wait("V", ev_c0, ev_prev_out[buf])
        i = V.tensor_tensor(s_out[buf][:, 0:512], s_wtmp[:, 0:512],
                            s_x[t % 4][:, 0:512], ALU.add)
        ev_f0 = em.sig(i, "sv")
        em.wait("V", ev_c1)
        i = V.tensor_tensor(s_out[buf][:, 512:1024], s_wtmp[:, 512:1024],
                            s_x[t % 4][:, 512:1024], ALU.add)
        ev_fin = em.sig(i, "sv")
        war_scr["wtmp"] = ev_fin
        war_x[t % 4] = ev_fin
        em.wait("Y", ev_fin)
        i = Y.dma_start(out=y_d[tok0:tok0 + 128, :], in_=s_out[buf])
        ev_prev_out[buf] = em.sig(i, f"so{buf}", 16)

        if t + 3 < NT:
            ev_x2 = emit_xload(NT + t + 3)
            ln_for_xnt = emit_ln(NT + t + 3, ev_x2)

        if t + 1 < NT:
            ev_L2n = emit_logits(t + 1, 1)
            th2 = emit_ffn_tanh(t + 1, ev_L2n, 1)
            av2 = emit_ffn_acts(t + 1, th2, 1)
            nxt["silu"].append(av2[0])
            war_Lp = [th2, av2[0]]
        xnTe_next = nxt.get("xnTe_next", ev_mTe0)
        cur = nxt

    em.wait("Y", ("so0", em.cnt["so0"]), ("so1", em.cnt["so1"]))
    return nc


def _prepare(inputs):
    x = np.ascontiguousarray(np.asarray(inputs["L0_attn_x"], dtype=np.float32))
    wq = np.asarray(inputs["L0_attn_Wq"], dtype=np.float32)
    wk = np.asarray(inputs["L0_attn_Wk"], dtype=np.float32)
    wv = np.asarray(inputs["L0_attn_Wv"], dtype=np.float32)
    wo = np.asarray(inputs["L0_attn_Wo"], dtype=np.float32)
    bq = np.asarray(inputs["L0_attn_bq"], dtype=np.float32)
    bk = np.asarray(inputs["L0_attn_bk"], dtype=np.float32)
    bv = np.asarray(inputs["L0_attn_bv"], dtype=np.float32)
    bo = np.asarray(inputs["L0_attn_bo"], dtype=np.float32)
    win = np.asarray(inputs["L0_ffn_W_in"], dtype=np.float32)
    bin_ = np.asarray(inputs["L0_ffn_b_in"], dtype=np.float32)
    wout = np.asarray(inputs["L0_ffn_W_out"], dtype=np.float32)

    perm = _tree_perm()
    wqt = np.ascontiguousarray(wq.T)
    wkt = np.ascontiguousarray(wk.T)
    wvt = np.ascontiguousarray(wv.T)
    wot = np.ascontiguousarray(wo.T)
    wint = np.ascontiguousarray(win[perm].T)
    woutt = np.ascontiguousarray(wout.T[perm])
    woutt = np.concatenate([woutt, bo[None, :]], axis=0)
    bqkv = np.concatenate([bq, bk, bv])[None, :]
    binp = np.ascontiguousarray(bin_[perm])[None, :]

    in_maps = []
    for c in range(NCORES):
        shard = np.ascontiguousarray(x[2 * c:2 * c + 2].reshape(NT * 128, D))
        in_maps.append({
            "x": shard, "wqt": wqt, "wkt": wkt, "wvt": wvt, "wot": wot,
            "wint": wint, "woutt": woutt, "bqkv": bqkv, "binp": binp,
        })
    return in_maps


def kernel(**inputs):
    in_maps = _prepare(inputs)
    nc = build_nc()
    res = run_bass_kernel_spmd(nc, in_maps, core_ids=list(range(NCORES)))
    outs = [np.asarray(r["y"]).reshape(2, SEQ, D) for r in res.results]
    return np.concatenate(outs, axis=0)


# version 29
# speedup vs baseline: 1.1902x; 1.1902x over previous
import numpy as np
import ml_dtypes

import concourse.bass as bass
import concourse.mybir as mybir
from concourse.bass_utils import run_bass_kernel_spmd

F32 = mybir.dt.float32
BF16 = mybir.dt.bfloat16
AF = mybir.ActivationFunctionType
ALU = mybir.AluOpType
AX = mybir.AxisListType

NCORES = 8
D = 1024
SEQ = 2048
TPB = 16
NB = 2
NT = TPB * NB
NH = 16
HD = 64
WIDTH = 2040
PAR = 8
NNODES = 255
DEPTH = 7
DECW = PAR * (2**DEPTH - 1)
EPS_LN = 1e-5
EPS_AT = 1e-6


def _tree_perm():
    perm = []
    ordd = [(t, 0) for t in range(PAR)]
    for d in range(DEPTH + 1):
        perm.extend(t * NNODES + (2**d - 1 + i) for (t, i) in ordd)
        ordd = [(t, 2 * i) for (t, i) in ordd] + [(t, 2 * i + 1) for (t, i) in ordd]
    return np.array(perm, dtype=np.int64)


class Em:
    def __init__(self, nc):
        self.nc = nc
        self.eng = {"T": nc.tensor, "V": nc.vector, "S": nc.scalar,
                    "Y": nc.sync, "G": nc.gpsimd}
        self.sems = {}
        self.cnt = {}
        self.seen = {e: {} for e in self.eng}
        self.war = {}

    def sem(self, name):
        h = self.nc.semaphore(name).__enter__()
        self.sems[name] = h
        self.cnt[name] = 0
        for e in self.seen:
            self.seen[e][name] = 0

    def wait(self, e, *evs):
        for ev in evs:
            if ev is None:
                continue
            s, v = ev
            if v <= self.seen[e][s]:
                continue
            self.eng[e].wait_ge(self.sems[s], v)
            self.seen[e][s] = v

    def sig(self, inst, s, k=1):
        self.cnt[s] += k
        inst.then_inc(self.sems[s], k)
        return (s, self.cnt[s])


def build_nc():
    nc = bass.Bass("TRN2", target_bir_lowering=False, debug=False)

    x_d = nc.dram_tensor("x", [NT * 128, D], F32, kind="ExternalInput").ap()
    y_d = nc.dram_tensor("y", [NT * 128, D], F32, kind="ExternalOutput").ap()
    wq_d = nc.dram_tensor("wqt", [D, D], F32, kind="ExternalInput").ap()
    wk_d = nc.dram_tensor("wkt", [D, D], F32, kind="ExternalInput").ap()
    wv_d = nc.dram_tensor("wvt", [D, D], F32, kind="ExternalInput").ap()
    wo_d = nc.dram_tensor("wot", [D, D], F32, kind="ExternalInput").ap()
    win_d = nc.dram_tensor("wint", [D, WIDTH], F32, kind="ExternalInput").ap()
    wout_d = nc.dram_tensor("woutt", [WIDTH + 1, D], F32, kind="ExternalInput").ap()
    bqkv_d = nc.dram_tensor("bqkv", [1, 3 * D], F32, kind="ExternalInput").ap()
    bin_d = nc.dram_tensor("binp", [1, WIDTH], F32, kind="ExternalInput").ap()

    ident_d = nc.inline_tensor(np.eye(128, dtype=ml_dtypes.bfloat16), name="identc").ap()
    allone_d = nc.inline_tensor(np.ones((128, 128), dtype=ml_dtypes.bfloat16), name="allonec").ap()

    al = nc.alloc_sbuf_tensor
    s_wkv = al("s_wkv", [128, 16384], BF16).ap()
    s_wq = al("s_wq", [128, 8192], BF16).ap()
    s_wo = al("s_wo", [128, 8192], BF16).ap()
    s_win = al("s_win", [128, 8 * WIDTH], BF16).ap()
    s_ident = al("s_ident", [128, 128], BF16).ap()
    s_allone = al("s_allone", [128, 128], BF16).ap()
    s_bq_b = al("s_bq_b", [128, D], BF16).ap()
    s_bk_b = al("s_bk_b", [128, D], BF16).ap()
    s_bv_b = al("s_bv_b", [128, D], BF16).ap()
    s_bin_b = al("s_bin_b", [128, WIDTH], BF16).ap()
    s_cns = al("s_cns", [128, 2], F32).ap()
    s_kvb = [al(f"s_kvb{b}", [128, 512], BF16).ap() for b in range(NB)]
    s_ksb = [al(f"s_ksb{b}", [128, D], F32).ap() for b in range(NB)]
    s_x = [al(f"s_x{i}", [128, D], F32).ap() for i in range(4)]
    s_xn = [al(f"s_xn{i}", [128, D], BF16).ap() for i in range(2)]
    s_xnt = [al(f"s_xnt{i}", [128, D], BF16).ap() for i in range(2)]
    s_lnp = al("s_lnp", [128, 16], F32).ap()
    s_stats = al("s_stats", [128, 2 * NT], F32).ap()
    s_scr1 = al("s_scr1", [128, D], F32).ap()
    s_scr2 = al("s_scr2", [128, D], F32).ap()
    s_phik = [al(f"s_phik{i}", [128, D], BF16).ap() for i in range(2)]
    s_v = [al(f"s_v{i}", [128, D], BF16).ap() for i in range(2)]
    s_phiq = al("s_phiq", [128, D], F32).ap()
    s_qp = al("s_qp", [128, D], BF16).ap()
    s_qpt = al("s_qpt", [128, D], BF16).ap()
    s_ap = al("s_ap", [128, D], BF16).ap()
    s_acts = [al(f"s_acts{i}", [128, WIDTH], BF16).ap() for i in range(2)]
    s_dn = [al(f"s_dn{i}", [128, 2040], BF16).ap() for i in range(2)]
    s_masked = al("s_masked", [128, WIDTH], BF16).ap()
    s_mT = al("s_mT", [128, 2048], BF16).ap()
    s_norm = al("s_norm", [128, 32], F32).ap()
    s_out = [al(f"s_out{i}", [128, D], F32).ap() for i in range(2)]

    em = Em(nc)
    for s in ("sc", "sx0", "sx1", "sx2", "sx3", "so0", "so1", "swbr",
              "swk", "swv", "swq", "swo", "swin", "swout",
              "st", "sv", "ss"):
        em.sem(s)
    T, V, S, Y, G = (em.eng[e] for e in "TVSYG")

    for dst, srcd in ((s_ident, ident_d), (s_allone, allone_d)):
        i = Y.dma_start(out=dst, in_=srcd)
        ev_consts = em.sig(i, "sc", 16)
    ev_ident = ev_allone = ev_consts

    def wdma(dst, src, sem):
        i = G.dma_start(out=dst, in_=src)
        return em.sig(i, sem, 16)

    wdma(s_bq_b[0:1, :], bqkv_d[0:1, 0:1024], "swbr")
    wdma(s_bk_b[0:1, :], bqkv_d[0:1, 1024:2048], "swbr")
    wdma(s_bv_b[0:1, :], bqkv_d[0:1, 2048:3072], "swbr")
    ev_brows = wdma(s_bin_b[0:1, :], bin_d, "swbr")

    def chunked(dst_cols0, src, ncols, nch, s_dst, sem):
        dst = s_dst[:, dst_cols0:dst_cols0 + nch * ncols].rearrange(
            "p (c j) -> p c j", c=nch)
        srcv = src.rearrange("(c p) j -> p c j", p=128)
        return wdma(dst, srcv, sem)

    ev_wk = chunked(0, wk_d, D, 8, s_wkv, "swk")
    ev_wv = chunked(8192, wv_d, D, 8, s_wkv, "swv")
    ev_wq = chunked(0, wq_d, D, 8, s_wq, "swq")
    ev_wo = chunked(0, wo_d, D, 8, s_wo, "swo")
    ev_win = chunked(0, win_d, WIDTH, 8, s_win, "swin")

    i = V.memset(s_mT[96:128, 1920:2048], 1.0)
    ev_mtones = em.sig(i, "sv")
    V.memset(s_cns[:, 0:1], 1.0 / D)
    i = V.memset(s_cns[:, 1:2], EPS_LN)
    ev_cns = em.sig(i, "sv")

    p_kq_g = nc.psum_tensor("p_kq", [128, D], F32); p_kq = p_kq_g.__enter__().ap()
    p_v0_g = nc.psum_tensor("p_v0", [128, 512], F32); p_v0 = p_v0_g.__enter__().ap()
    p_kv_g = nc.psum_tensor("p_kv", [128, D], F32); p_kv = p_kv_g.__enter__().ap()
    p_ksb_g = nc.psum_tensor("p_ksb", [128, D], F32); p_ksb = p_ksb_g.__enter__().ap()
    p_xnt_g = nc.psum_tensor("p_xnt", [128, D], BF16); p_xnt = p_xnt_g.__enter__().ap()

    ev_bc_v = None
    for dst, c0, c1 in ((s_bq_b, 0, 1024), (s_bk_b, 0, 1024), (s_bv_b, 0, 1024),
                        (s_bin_b, 0, 1024), (s_bin_b, 1024, 2040)):
        em.wait("T", ev_brows, ev_consts, ev_bc_v)
        w = c1 - c0
        T.matmul(p_kq[:, 0:512], s_allone[0:1, :], dst[0:1, c0:c0 + 512],
                 start=True, stop=True)
        i = T.matmul(p_kq[:, 512:512 + (w - 512)], s_allone[0:1, :],
                     dst[0:1, c0 + 512:c1], start=True, stop=True)
        ev_bc_mm = em.sig(i, "st")
        em.wait("V", ev_bc_mm)
        i = V.tensor_copy(dst[:, c0:c1], p_kq[:, 0:w])
        ev_bc_v = em.sig(i, "sv")
    ev_biasb = ev_bc_v

    war_x = [None] * 4
    war_xn = [None, None]
    war_xnt = [None, None]
    war_ln = [None, None]

    def emit_xload(t):
        xb = t % 4
        tok0 = (t % NT) * 128
        em.wait("Y", war_x[xb])
        i = Y.dma_start(out=s_x[xb], in_=x_d[tok0:tok0 + 128, :])
        return em.sig(i, f"sx{xb}", 16)

    def emit_ln(t, ev_x):
        buf = t % 2
        xb = t % 4
        i_t = t % NT
        em.wait("V", ev_x, war_xn[buf])
        i = V.tensor_scalar(s_xn[buf], s_x[xb], s_stats[:, 2 * i_t:2 * i_t + 1],
                            s_stats[:, 2 * i_t + 1:2 * i_t + 2],
                            ALU.subtract, ALU.mult)
        ev = em.sig(i, "sv")
        war_x[xb] = ev
        return ev

    war_lnp = [None]

    def emit_sums(t, ev_x):
        p = t % 2
        xb = t % 4
        em.wait("S", ev_x, em.war.get(("sq",)), war_lnp[0])
        i = S.activation(s_phiq, s_x[xb], AF.Square,
                         accum_out=s_lnp[:, 2 + p:3 + p])
        ev_ss = em.sig(i, "ss")
        em.war[("sq",)] = ev_ss
        em.wait("V", ev_x)
        i = V.reduce_sum(s_lnp[:, p:p + 1], s_x[xb], axis=AX.X)
        ev_sv = em.sig(i, "sv")
        return (ev_sv, ev_ss)

    def emit_pair(u, se0, se1):
        pstep = s_stats.ap[0][0]
        mean_ap = bass.AP(tensor=s_stats.tensor,
                          offset=s_stats.offset + 2 * u,
                          ap=[[pstep, 128], [2, 2]])
        rstd_ap = bass.AP(tensor=s_stats.tensor,
                          offset=s_stats.offset + 2 * u + 1,
                          ap=[[pstep, 128], [2, 2]])
        em.wait("V", se0[0], se0[1], se1[0], se1[1])
        i = V.tensor_scalar(mean_ap, s_lnp[:, 0:2], 1.0 / D, None, ALU.mult)
        e = em.sig(i, "sv")
        em.wait("V", e)
        i = V.tensor_tensor(s_lnp[:, 4:6], s_lnp[:, 0:2], mean_ap, ALU.mult)
        e = em.sig(i, "sv")
        em.wait("V", e, em.war.get(("lnp68",)))
        i = V.tensor_tensor(s_lnp[:, 6:8], s_lnp[:, 2:4], s_lnp[:, 4:6],
                            ALU.subtract)
        e3 = em.sig(i, "sv")
        war_lnp[0] = e3
        em.wait("S", e3, ev_cns)
        i = S.activation(s_lnp[:, 8:10], s_lnp[:, 6:8], AF.Sqrt,
                         scale=s_cns[:, 0:1], bias=s_cns[:, 1:2])
        e4 = em.sig(i, "ss")
        em.war[("lnp68",)] = e4
        em.wait("V", e4)
        i = V.reciprocal(rstd_ap, s_lnp[:, 8:10])
        return em.sig(i, "sv")

    def emit_xnstt(u, ev_y):
        xb = u % 4
        buf = u % 2
        em.wait("V", ev_y, war_xn[buf])
        i = V.tensor_scalar(s_xn[buf], s_x[xb],
                            s_stats[:, 2 * u:2 * u + 1],
                            s_stats[:, 2 * u + 1:2 * u + 2],
                            ALU.subtract, ALU.mult)
        ev = em.sig(i, "sv")
        war_x[xb] = ev
        return ev

    def emit_xnt(t, ev_xn, war_bank, p_dst):
        buf = t % 2
        em.wait("T", ev_xn, ev_ident, war_bank)
        for c in range(8):
            i = T.transpose(p_dst[:, 128 * c:128 * (c + 1)],
                            s_xn[buf][:, 128 * c:128 * (c + 1)], s_ident)
        ev_tr = em.sig(i, "st")
        war_xn[buf] = ev_tr
        war_ln[buf] = ev_tr
        em.wait("S", ev_tr, war_xnt[buf])
        i = S.copy(s_xnt[buf], p_dst[:, 0:1024])
        ev = em.sig(i, "ss")
        return ev, ev_tr

    kv_evac = [None, None]
    ksb_evac = [None, None]
    war_phik = [None, None]
    war_v = [None, None]
    war_pxnt = None
    pending_kv = None

    ev_xl = [None] * NT
    sums_ev = [None] * NT
    ev_stt = [None] * NT
    ev_xl[0] = emit_xload(0)
    sums_ev[0] = emit_sums(0, ev_xl[0])
    ev_xl[1] = emit_xload(1)
    sums_ev[1] = emit_sums(1, ev_xl[1])
    ev_y = emit_pair(0, sums_ev[0], sums_ev[1])
    ev_stt[0] = emit_xnstt(0, ev_y)
    ev_stt[1] = emit_xnstt(1, ev_y)
    ev_xnTe, _ = emit_xnt(0, ev_stt[0], None, p_xnt)
    war_pxnt = ev_xnTe
    ev_xl[2] = emit_xload(2)
    sums_ev[2] = emit_sums(2, ev_xl[2])
    ev_xl[3] = emit_xload(3)

    def emit_kv(pend):
        tt, pbuf, ev_pk, ev_pv = pend
        bb = tt // TPB
        firstt = (tt % TPB) == 0
        lastt = (tt % TPB) == TPB - 1
        em.wait("T", ev_pk, ev_pv)
        if firstt and bb == 1:
            em.wait("T", kv_evac[0], ksb_evac[0])
        for h in range(NH):
            a = h % 2
            j = h // 2
            i = T.matmul(p_kv[64 * a:64 * a + 64,
                              512 * a + 64 * j:512 * a + 64 * (j + 1)],
                         s_phik[pbuf][:, 64 * h:64 * h + 64],
                         s_v[pbuf][:, 64 * h:64 * h + 64],
                         start=(firstt and h < 2), stop=(lastt and h >= NH - 2),
                         tile_position=(0, 64 * a))
        ev_kvv = em.sig(i, "st")
        em.wait("T", ev_allone)
        T.matmul(p_ksb[:, 0:512], s_allone, s_phik[pbuf][:, 0:512],
                 start=firstt, stop=lastt)
        i = T.matmul(p_ksb[:, 512:1024], s_allone, s_phik[pbuf][:, 512:1024],
                     start=firstt, stop=lastt)
        ev_ksbb = em.sig(i, "st")
        war_phik[pbuf] = ev_ksbb
        war_v[pbuf] = ev_kvv
        if lastt:
            em.wait("V", ev_kvv)
            V.tensor_copy(s_kvb[bb][0:64, 0:512], p_kv[0:64, 0:512])
            i = V.tensor_copy(s_kvb[bb][64:128, 0:512], p_kv[64:128, 512:1024])
            kv_evac[bb] = em.sig(i, "sv")
            em.wait("V", ev_ksbb)
            i = V.tensor_copy(s_ksb[bb], p_ksb)
            ksb_evac[bb] = em.sig(i, "sv")

    BNK = [p_kq[:, 0:512], p_kq[:, 512:1024], p_v0]
    war_bank = [ev_biasb, ev_biasb, None]
    ev_last_kvmm = None
    for t in range(NT):
        buf = t % 2
        r0, r1, r2 = t % 3, (t + 1) % 3, (t + 2) % 3

        em.wait("T", ev_xnTe, ev_wk, war_bank[r0], war_bank[r1])
        ev_kh = []
        for half, bk in ((0, BNK[r0]), (1, BNK[r1])):
            for c in range(8):
                i = T.matmul(bk, s_xnt[buf][:, 128 * c:128 * (c + 1)],
                             s_wkv[:, 1024 * c + 512 * half:1024 * c + 512 * (half + 1)],
                             start=(c == 0), stop=(c == 7))
            ev_kh.append(em.sig(i, "st"))
        em.wait("V", ev_kh[0], ev_biasb, em.war.get(("scr1",)))
        i = V.tensor_tensor(s_scr1[:, 0:512], BNK[r0], s_bk_b[:, 0:512], ALU.add)
        ev_kb0 = em.sig(i, "sv")
        war_bank[r0] = ev_kb0
        em.wait("V", ev_kh[1])
        i = V.tensor_tensor(s_scr1[:, 512:1024], BNK[r1], s_bk_b[:, 512:1024],
                            ALU.add)
        ev_kb1 = em.sig(i, "sv")
        war_bank[r1] = ev_kb1

        if pending_kv is not None:
            emit_kv(pending_kv)
            pending_kv = None

        if t + 1 < NT:
            ev_xnTe_next, _ = emit_xnt(t + 1, ev_stt[t + 1], war_pxnt, p_xnt)
            war_pxnt = ev_xnTe_next

        em.wait("T", ev_wv, war_bank[r2])
        for c in range(8):
            i = T.matmul(BNK[r2], s_xnt[buf][:, 128 * c:128 * (c + 1)],
                         s_wkv[:, 8192 + 1024 * c:8192 + 1024 * c + 512],
                         start=(c == 0), stop=(c == 7))
        ev_vh0 = em.sig(i, "st")
        em.wait("T", war_bank[r0])
        for c in range(8):
            i = T.matmul(BNK[r0], s_xnt[buf][:, 128 * c:128 * (c + 1)],
                         s_wkv[:, 8192 + 1024 * c + 512:8192 + 1024 * (c + 1)],
                         start=(c == 0), stop=(c == 7))
        ev_vh1 = em.sig(i, "st")
        ev_vh = [ev_vh0, ev_vh1]
        war_xnt[buf] = ev_vh1
        ev_last_kvmm = ev_vh1

        em.wait("S", ev_kb1, em.war.get(("scr2",)))
        i = S.activation(s_scr2, s_scr1, AF.Relu, scale=-1.0)
        ev_e1 = em.sig(i, "ss")
        em.wait("S", ev_e1)
        i = S.activation(s_scr2, s_scr2, AF.Exp, scale=-1.0)
        ev_e2 = em.sig(i, "ss")

        em.wait("V", ev_vh[0], war_v[buf])
        i = V.tensor_tensor(s_v[buf][:, 0:512], BNK[r2], s_bv_b[:, 0:512],
                            ALU.add)
        war_bank[r2] = em.sig(i, "sv")
        em.wait("V", ev_vh[1])
        i = V.tensor_tensor(s_v[buf][:, 512:1024], BNK[r0], s_bv_b[:, 512:1024],
                            ALU.add)
        ev_vb = em.sig(i, "sv")
        war_bank[r0] = ev_vb

        em.wait("V", ev_e2, war_phik[buf])
        i = V.scalar_tensor_tensor(s_phik[buf], s_scr1, 0.0, s_scr2,
                                   ALU.max, ALU.add)
        ev_phik = em.sig(i, "sv")
        em.war[("scr1",)] = ev_phik
        em.war[("scr2",)] = ev_phik
        pending_kv = (t, buf, ev_phik, ev_vb)

        if t + 3 < NT:
            sums_ev[t + 3] = emit_sums(t + 3, ev_xl[t + 3])
            if t % 2 == 0:
                ev_y = emit_pair(t + 2, sums_ev[t + 2], sums_ev[t + 3])
                ev_stt[t + 2] = emit_xnstt(t + 2, ev_y)
                ev_stt[t + 3] = emit_xnstt(t + 3, ev_y)

        if t + 1 < NT:
            ev_xnTe = ev_xnTe_next
        if t + 4 < NT:
            ev_xl[t + 4] = emit_xload(t + 4)

    emit_kv(pending_kv)
    pending_kv = None

    em.wait("G", ev_last_kvmm)
    chunked(0, wout_d[0:1920, :], D, 15, s_wkv, "swout")
    i = G.dma_start(out=s_wkv[0:121, 15360:16384], in_=wout_d[1920:2041, :])
    ev_wout = em.sig(i, "swout", 16)

    p_xnt_g.__exit__(None, None, None)
    p_ksb_g.__exit__(None, None, None)
    p_kv_g.__exit__(None, None, None)
    p_v0_g.__exit__(None, None, None)
    p_kq_g.__exit__(None, None, None)

    p_acc = [nc.psum_tensor(f"p_acc{i}", [128, D], F32).__enter__().ap()
             for i in range(2)]
    p_L = nc.psum_tensor("p_L", [128, D], F32).__enter__().ap()
    p_tr = nc.psum_tensor("p_tr", [128, 2048], BF16).__enter__().ap()

    em.wait("T", war_bank[0], war_bank[1], war_bank[2], kv_evac[1],
            ksb_evac[1], war_pxnt,
            war_phik[0], war_phik[1], war_v[0], war_v[1])
    war_acc = [[], []]
    war_Lp = []
    ev_prev_out = [None, None]
    war_scr = {"scr2": em.war.get(("scr2",)),
               "phiq": em.war.get(("sq",))}
    _rn = s_norm[:, 16:32]
    rn_ap = bass.AP(tensor=_rn.tensor, offset=_rn.offset,
                    ap=[[_rn.ap[0][0], 128], [_rn.ap[1][0], NH], [0, HD]])
    srcL = bass.AP(tensor=p_L.tensor, offset=p_L.offset,
                   ap=[[p_L.ap[0][0], 128], [512, 2], [1, 510]])

    def emit_qmm(t, ev_xnTe_t):
        acc = p_acc[t % 2]
        buf = t % 2
        em.wait("T", ev_xnTe_t, ev_wq, *war_acc[t % 2])
        for c in range(8):
            lhs = s_xnt[buf][:, 128 * c:128 * (c + 1)]
            T.matmul(acc[:, 0:512], lhs, s_wq[:, 1024 * c:1024 * c + 512],
                     start=(c == 0), stop=(c == 7))
            i = T.matmul(acc[:, 512:1024], lhs, s_wq[:, 1024 * c + 512:1024 * (c + 1)],
                         start=(c == 0), stop=(c == 7))
        return em.sig(i, "st")

    def emit_qbias(t, ev_mm):
        acc = p_acc[t % 2]
        em.wait("V", ev_mm, ev_biasb)
        i = V.tensor_tensor(acc, acc, s_bq_b, ALU.add)
        return em.sig(i, "sv")

    def emit_logits(t, phase):
        buf = t % 2
        em.wait("T", ev_win, *war_Lp)
        for c in range(8):
            lhs = s_xnt[buf][:, 128 * c:128 * (c + 1)]
            for k4 in (0, 1):
                kk = 2 * phase + k4
                T.matmul(p_L[:, 512 * k4:512 * k4 + 510], lhs,
                         s_win[:, WIDTH * c + 510 * kk:WIDTH * c + 510 * (kk + 1)],
                         start=(c == 0), stop=False)
        for k4 in (0, 1):
            kk = 2 * phase + k4
            i = T.matmul(p_L[:, 512 * k4:512 * k4 + 510], s_allone[0:1, :],
                         s_bin_b[0:1, 510 * kk:510 * (kk + 1)],
                         start=False, stop=True)
        return em.sig(i, "st")

    def emit_ffn_tanh(t, ev_Lmm, phase):
        pb = t % 2
        em.wait("S", ev_Lmm, war_scr.get("acts"))
        i = S.activation(
            s_acts[pb][:, 1020 * phase:1020 * (phase + 1)].rearrange(
                "p (b c) -> p b c", b=2), srcL, AF.Tanh, scale=0.5)
        return em.sig(i, "ss")

    def emit_ffn_acts(t, ev_th, phase):
        pb = t % 2
        evs = []
        if phase == 0:
            em.wait("V", ev_th, war_scr.get("dec"))
            i = V.tensor_scalar(s_dn[pb][:, 1020:2040], s_acts[pb][:, 0:1020],
                                0.0, None, ALU.is_gt)
            ev_dec = em.sig(i, "sv")
            em.wait("V", war_scr.get("nd"))
            i = V.tensor_scalar(s_dn[pb][:, 0:1020], s_acts[pb][:, 0:1020],
                                0.0, None, ALU.is_le)
            ev_nd = em.sig(i, "sv")
            evs.extend([ev_dec, ev_nd])
        em.wait("V", ev_th)
        sl = s_acts[pb][:, 1020 * phase:1020 * (phase + 1)].rearrange(
            "p (b c) -> p b c", b=2)
        i = V.scalar_tensor_tensor(sl, sl, 1.0, srcL, ALU.add, ALU.mult)
        ev_acts = em.sig(i, "sv")
        return [ev_acts] + evs

    def emit_echain(t, ev_qmm_t):
        em.wait("S", ev_qmm_t, war_scr.get("scr2"))
        i = S.activation(s_scr2, p_acc[t % 2], AF.Relu, scale=-1.0)
        ev_e1 = em.sig(i, "ss")
        em.wait("S", ev_e1)
        i = S.activation(s_scr2, s_scr2, AF.Exp, scale=-1.0)
        return em.sig(i, "ss")

    def rn_ap_h(h5):
        return bass.AP(tensor=s_norm.tensor, offset=s_norm.offset + 16 + 8 * h5,
                       ap=[[s_norm.ap[0][0], 128], [1, 8], [0, HD]])

    def emit_phiq_pair(t, ev_e2, ev_qb):
        acc = p_acc[t % 2]
        bb = t // TPB
        ev = {}
        em.wait("V", ev_e2, ev_qb, war_scr.get("phiq"))
        for h5 in (0, 1):
            hc = 512 * h5
            i = V.scalar_tensor_tensor(s_phiq[:, hc:hc + 512],
                                       acc[:, hc:hc + 512], 0.0,
                                       s_scr2[:, hc:hc + 512],
                                       ALU.max, ALU.add)
            ev[("phiq", h5)] = em.sig(i, "sv")
        war_scr["scr2"] = ev[("phiq", 1)]
        for h5 in (0, 1):
            hc = 512 * h5
            em.wait("V", ev[("phiq", h5)], ksb_evac[bb],
                    em.war.get(("scr1",)))
            i = V.tensor_tensor(s_scr1[:, hc:hc + 512], s_phiq[:, hc:hc + 512],
                                s_ksb[bb][:, hc:hc + 512], ALU.mult)
            ev[("np", h5)] = em.sig(i, "sv")
        for h5 in (0, 1):
            hc = 512 * h5
            em.wait("V", ev[("np", h5)])
            i = V.reduce_sum(
                s_norm[:, 8 * h5:8 * h5 + 8].rearrange("p (h o) -> p h o", o=1),
                s_scr1[:, hc:hc + 512].rearrange("p (h d) -> p h d", h=8),
                axis=AX.X)
            ev[("seg", h5)] = em.sig(i, "sv")
        em.war[("scr1",)] = ev[("seg", 1)]
        em.wait("V", ev[("seg", 0)], ev[("seg", 1)])
        i = V.tensor_scalar(s_norm[:, 16:32], s_norm[:, 0:16], EPS_AT, None,
                            ALU.max)
        e = em.sig(i, "sv")
        em.wait("V", e)
        i = V.reciprocal(s_norm[:, 16:32], s_norm[:, 16:32])
        ev_rn = em.sig(i, "sv")
        em.wait("V", ev_rn, war_scr.get("qp"))
        qp_ev = []
        for h5 in (0, 1):
            hc = 512 * h5
            i = V.tensor_tensor(
                s_qp[:, hc:hc + 512].rearrange("p (h d) -> p h d", h=8),
                s_phiq[:, hc:hc + 512].rearrange("p (h d) -> p h d", h=8),
                rn_ap_h(h5), ALU.mult)
            qp_ev.append(em.sig(i, "sv"))
        war_scr["phiq"] = qp_ev[1]
        return {"phiq0": ev[("phiq", 0)], "qp0": qp_ev[0],
                "phiq1": ev[("phiq", 1)], "qp1": qp_ev[1]}

    def emit_mask_levels(t, dec_nd):
        pb = t % 2
        dnb = s_dn[pb]
        pstep = s_masked.ap[0][0]
        dnstep = dnb.ap[0][0]
        em.wait("V", *dec_nd)
        em.wait("V", war_scr.get("masked"))
        i = V.tensor_scalar(
            bass.AP(tensor=s_masked.tensor, offset=s_masked.offset + 8,
                    ap=[[pstep, 128], [8, 2], [1, 8]]),
            bass.AP(tensor=dnb.tensor, offset=dnb.offset,
                    ap=[[dnstep, 128], [1020, 2], [1, 8]]),
            0.5, None, ALU.mult)
        ev_m = em.sig(i, "sv")
        for d in range(1, DEPTH):
            sdd = PAR * 2**d
            o = PAR * (2**d - 1)
            o2 = PAR * (2**(d + 1) - 1)
            em.wait("V", ev_m)
            i = V.tensor_tensor(
                bass.AP(tensor=s_masked.tensor, offset=s_masked.offset + o2,
                        ap=[[pstep, 128], [sdd, 2], [1, sdd]]),
                bass.AP(tensor=s_masked.tensor, offset=s_masked.offset + o,
                        ap=[[pstep, 128], [0, 2], [1, sdd]]),
                bass.AP(tensor=dnb.tensor, offset=dnb.offset + o,
                        ap=[[dnstep, 128], [1020, 2], [1, sdd]]),
                ALU.mult)
            ev_m = em.sig(i, "sv")
        return ev_m

    def emit_mask_final(t, ev_lev, silu_evs):
        pb = t % 2
        actsb = s_acts[pb]
        em.wait("V", ev_lev, *silu_evs)
        i = V.tensor_tensor(s_masked[:, 8:WIDTH], actsb[:, 8:WIDTH],
                            s_masked[:, 8:WIDTH], ALU.mult)
        e = em.sig(i, "sv")
        em.wait("V", e)
        i = V.tensor_scalar(s_masked[:, 0:8], actsb[:, 0:8], 0.5, None, ALU.mult)
        ev_mk = em.sig(i, "sv")
        war_scr["acts"] = ev_mk
        war_scr["dec"] = ev_mk
        war_scr["nd"] = ev_mk
        return ev_mk

    ev_x = emit_xload(NT + 0)
    ln_cur = emit_ln(NT + 0, ev_x)
    ev_xnTe0, _ = emit_xnt(NT + 0, ln_cur, None, p_tr)
    ev_x2 = emit_xload(NT + 1)
    ln1 = emit_ln(NT + 1, ev_x2)
    ev_qmm0 = emit_qmm(0, ev_xnTe0)
    ev_L1 = emit_logits(0, 0)
    ev_qb0 = emit_qbias(0, ev_qmm0)
    ev_e2_0 = emit_echain(0, ev_qb0)
    th1 = emit_ffn_tanh(0, ev_L1, 0)
    av1 = emit_ffn_acts(0, th1, 0)
    war_Lp = [th1, av1[0]]
    ev_L2 = emit_logits(0, 1)
    th2 = emit_ffn_tanh(0, ev_L2, 1)
    av2 = emit_ffn_acts(0, th2, 1)
    war_Lp = [th2, av2[0]]
    mlev0 = emit_mask_levels(0, av1[1:])
    phi0 = emit_phiq_pair(0, ev_e2_0, ev_qb0)
    xnTe_next, _ = emit_xnt(NT + 1, ln1, ev_xnTe0, p_tr)
    ev_x3 = emit_xload(NT + 2)
    ln_for_xnt = emit_ln(NT + 2, ev_x3)
    cur = dict(ev_e2=ev_e2_0, ev_qb=ev_qb0, mlev=mlev0,
               silu=[av1[0], av2[0]], **phi0)
    war_trhi = None

    for t in range(NT):
        b = t // TPB
        buf = t % 2
        tok0 = t * 128
        acc = p_acc[t % 2]

        nxt = {}
        if t + 1 < NT:
            nxt["ev_qmm"] = emit_qmm(t + 1, xnTe_next)
            nxt["_L1"] = emit_logits(t + 1, 0)

        if t > 0:
            cur.update(emit_phiq_pair(t, cur["ev_e2"], cur["ev_qb"]))
        ev_phiq0 = cur["phiq0"]; ev_qp0 = cur["qp0"]
        ev_phiq1 = cur["phiq1"]; ev_qp1 = cur["qp1"]
        ev_mk = emit_mask_final(t, cur["mlev"], cur["silu"])

        em.wait("T", ev_qp0, xnTe_next)
        for c in range(4):
            i = T.transpose(p_tr[:, 128 * c:128 * (c + 1)],
                            s_qp[:, 128 * c:128 * (c + 1)], s_ident)
        ev_qpT0 = em.sig(i, "st")
        em.wait("T", ev_qp1, war_trhi)
        for c in range(4, 8):
            i = T.transpose(p_tr[:, 1024 + 128 * (c - 4):1024 + 128 * (c - 3)],
                            s_qp[:, 128 * c:128 * (c + 1)], s_ident)
        ev_qpT1 = em.sig(i, "st")
        war_scr["qp"] = ev_qpT1
        em.wait("S", ev_qpT0, war_scr.get("qpt"))
        i = S.copy(s_qpt[:, 0:512], p_tr[:, 0:512])
        ev_qpTe0 = em.sig(i, "ss")
        em.wait("S", ev_qpT1)
        i = S.copy(s_qpt[:, 512:1024], p_tr[:, 1024:1536])
        ev_qpTe1 = em.sig(i, "ss")

        em.wait("T", ev_qpTe0, kv_evac[b], ev_phiq0)
        for h in range(0, 8):
            a = h % 2
            j = h // 2
            i = T.matmul(acc[64 * a:64 * a + 64, 128 * j:128 * (j + 1)],
                         s_kvb[b][64 * a:64 * a + 64, 64 * j:64 * j + 64],
                         s_qpt[64 * a:64 * a + 64, 128 * j:128 * (j + 1)],
                         start=True, stop=True, tile_position=(64 * a, 64 * a))
        ev_ap0 = em.sig(i, "st")
        em.wait("T", ev_qpTe1, ev_phiq1)
        for h in range(8, NH):
            a = h % 2
            j = h // 2
            i = T.matmul(acc[64 * a:64 * a + 64, 128 * j:128 * (j + 1)],
                         s_kvb[b][64 * a:64 * a + 64, 64 * j:64 * j + 64],
                         s_qpt[64 * a:64 * a + 64, 128 * j:128 * (j + 1)],
                         start=True, stop=True, tile_position=(64 * a, 64 * a))
        ev_ap1 = em.sig(i, "st")
        war_scr["qpt"] = ev_ap1
        em.wait("S", ev_ap0, war_scr.get("ap"))
        i = S.copy(s_ap[:, 0:512], acc[:, 0:512])
        em.sig(i, "ss")
        em.wait("S", ev_ap1)
        i = S.copy(s_ap[:, 512:1024], acc[:, 512:1024])
        ev_ape = em.sig(i, "ss")

        em.wait("T", ev_mk, ev_qpTe1)
        for jc in range(15):
            i = T.transpose(p_tr[:, 128 * jc:128 * (jc + 1)],
                            s_masked[:, 128 * jc:128 * (jc + 1)], s_ident)
        i = T.transpose(p_tr[0:120, 1920:2048], s_masked[:, 1920:WIDTH], s_ident)
        ev_mTmm = em.sig(i, "st")
        war_scr["masked"] = ev_mTmm
        em.wait("S", ev_mTmm, ev_mtones, war_scr.get("mT"))
        i = S.copy(s_mT[:, 0:1024], p_tr[:, 0:1024])
        ev_mTe0 = em.sig(i, "ss")
        S.copy(s_mT[:, 1024:1920], p_tr[:, 1024:1920])
        i = S.copy(s_mT[0:120, 1920:2048], p_tr[0:120, 1920:2048])
        ev_mTe1 = em.sig(i, "ss")
        war_trhi = ev_mTe1

        em.wait("T", ev_ape, ev_wo)
        for c in range(8):
            lhs = s_ap[:, 128 * c:128 * (c + 1)]
            T.matmul(acc[:, 0:512], lhs, s_wo[:, 1024 * c:1024 * c + 512],
                     start=(c == 0), stop=False)
            i = T.matmul(acc[:, 512:1024], lhs, s_wo[:, 1024 * c + 512:1024 * (c + 1)],
                         start=(c == 0), stop=False)
        ev_womm = em.sig(i, "st")
        war_scr["ap"] = ev_womm

        if t + 1 < NT:
            nxt["ev_qb"] = emit_qbias(t + 1, nxt["ev_qmm"])
            nxt["_th1"] = emit_ffn_tanh(t + 1, nxt["_L1"], 0)
            nxt["ev_e2"] = emit_echain(t + 1, nxt["ev_qb"])

        if t + 1 < NT:
            av = emit_ffn_acts(t + 1, nxt["_th1"], 0)
            nxt["silu"] = [av[0]]
            war_Lp = [nxt["_th1"], av[0]]
            nxt["mlev"] = emit_mask_levels(t + 1, av[1:])

        if t + 2 < NT:
            nxt["xnTe_next"], _ = emit_xnt(NT + t + 2, ln_for_xnt, ev_mTe0, p_tr)

        em.wait("T", ev_mTe0, ev_wout)
        ev_oh = []
        for half in range(2):
            for j in range(16):
                if j == 8:
                    em.wait("T", ev_mTe1)
                kj = 128 if j < 15 else 121
                lhs = s_mT[0:kj, 128 * j:128 * (j + 1)]
                i = T.matmul(acc[:, 512 * half:512 * (half + 1)], lhs,
                             s_wkv[0:kj, 1024 * j + 512 * half:
                                   1024 * j + 512 * (half + 1)],
                             start=False, stop=(j == 15))
            ev_oh.append(em.sig(i, "st"))
        war_scr["mT"] = ev_oh[1]

        em.wait("V", ev_oh[0], ev_prev_out[buf])
        i = V.tensor_tensor(s_out[buf][:, 0:512], acc[:, 0:512],
                            s_x[t % 4][:, 0:512], ALU.add)
        ev_f0 = em.sig(i, "sv")
        em.wait("V", ev_oh[1])
        i = V.tensor_tensor(s_out[buf][:, 512:1024], acc[:, 512:1024],
                            s_x[t % 4][:, 512:1024], ALU.add)
        ev_fin = em.sig(i, "sv")
        war_acc[t % 2] = [ev_f0, ev_fin]
        war_x[t % 4] = ev_fin
        em.wait("Y", ev_fin)
        i = Y.dma_start(out=y_d[tok0:tok0 + 128, :], in_=s_out[buf])
        ev_prev_out[buf] = em.sig(i, f"so{buf}", 16)

        if t + 3 < NT:
            ev_x2 = emit_xload(NT + t + 3)
            ln_for_xnt = emit_ln(NT + t + 3, ev_x2)

        if t + 1 < NT:
            ev_L2n = emit_logits(t + 1, 1)
            th2 = emit_ffn_tanh(t + 1, ev_L2n, 1)
            av2 = emit_ffn_acts(t + 1, th2, 1)
            nxt["silu"].append(av2[0])
            war_Lp = [th2, av2[0]]
        xnTe_next = nxt.get("xnTe_next", ev_mTe0)
        cur = nxt

    em.wait("Y", ("so0", em.cnt["so0"]), ("so1", em.cnt["so1"]))
    return nc


def _prepare(inputs):
    x = np.ascontiguousarray(np.asarray(inputs["L0_attn_x"], dtype=np.float32))
    wq = np.asarray(inputs["L0_attn_Wq"], dtype=np.float32)
    wk = np.asarray(inputs["L0_attn_Wk"], dtype=np.float32)
    wv = np.asarray(inputs["L0_attn_Wv"], dtype=np.float32)
    wo = np.asarray(inputs["L0_attn_Wo"], dtype=np.float32)
    bq = np.asarray(inputs["L0_attn_bq"], dtype=np.float32)
    bk = np.asarray(inputs["L0_attn_bk"], dtype=np.float32)
    bv = np.asarray(inputs["L0_attn_bv"], dtype=np.float32)
    bo = np.asarray(inputs["L0_attn_bo"], dtype=np.float32)
    win = np.asarray(inputs["L0_ffn_W_in"], dtype=np.float32)
    bin_ = np.asarray(inputs["L0_ffn_b_in"], dtype=np.float32)
    wout = np.asarray(inputs["L0_ffn_W_out"], dtype=np.float32)

    perm = _tree_perm()
    wqt = np.ascontiguousarray(wq.T)
    wkt = np.ascontiguousarray(wk.T)
    wvt = np.ascontiguousarray(wv.T)
    wot = np.ascontiguousarray(wo.T)
    wint = np.ascontiguousarray(win[perm].T)
    woutt = np.ascontiguousarray(wout.T[perm])
    woutt = np.concatenate([woutt, bo[None, :]], axis=0)
    bqkv = np.concatenate([bq, bk, bv])[None, :]
    binp = np.ascontiguousarray(bin_[perm])[None, :]

    in_maps = []
    for c in range(NCORES):
        shard = np.ascontiguousarray(x[2 * c:2 * c + 2].reshape(NT * 128, D))
        in_maps.append({
            "x": shard, "wqt": wqt, "wkt": wkt, "wvt": wvt, "wot": wot,
            "wint": wint, "woutt": woutt, "bqkv": bqkv, "binp": binp,
        })
    return in_maps


def kernel(**inputs):
    in_maps = _prepare(inputs)
    nc = build_nc()
    res = run_bass_kernel_spmd(nc, in_maps, core_ids=list(range(NCORES)))
    outs = [np.asarray(r["y"]).reshape(2, SEQ, D) for r in res.results]
    return np.concatenate(outs, axis=0)
